# version 02 (final)
# baseline (speedup 1.0000x reference)
import contextlib
import numpy as np

import concourse.bass as bass
import concourse.mybir as mybir
from concourse.bass_utils import run_bass_kernel_spmd

F32 = mybir.dt.float32
BF16 = mybir.dt.bfloat16
AX = mybir.AxisListType.X
AF = mybir.ActivationFunctionType
ALU = mybir.AluOpType

D = 1024
EXP = 8
II = 2048
SCALE = 1.0 / 32.0
NBUF = 4
ENGS = ("tensor", "vector", "scalar", "sync", "gpsimd")


class K:

    def __init__(self):
        self.plan = {e: [] for e in ENGS}
        self.cnt = {}
        self.sems = {}

    def sem(self, name, stack, nc):
        self.sems[name] = stack.enter_context(nc.semaphore(name))
        self.cnt[name] = 0

    def w(self, eng, evt):
        if evt is None:
            return
        s, v = evt
        sem = self.sems[s]
        self.plan[eng].append(lambda E, sem=sem, v=v: E.wait_ge(sem, v))

    def wmax(self, eng, evts):
        by = {}
        for e in evts:
            if e is None:
                continue
            by[e[0]] = max(by.get(e[0], 0), e[1])
        for s, v in by.items():
            self.w(eng, (s, v))

    def e(self, eng, fn, inc=None, amt=1):
        if eng == "vector":
            self.w("vector", ("ev", self.cnt["ev"]))
            if inc is None:
                inc, amt = "ev", 1
        if inc is None:
            self.plan[eng].append(fn)
            return None
        self.cnt[inc] += amt
        v = self.cnt[inc]
        sem = self.sems[inc]
        self.plan[eng].append(lambda E, fn=fn, sem=sem, amt=amt: fn(E).then_inc(sem, amt))
        return (inc, v)


class Banks:

    def __init__(self, k, ap, nbanks, width):
        self.k, self.ap, self.n, self.w = k, ap, nbanks, width
        self.free = [[] for _ in range(nbanks)]
        self.rr = 0

    def acq(self, cols=None):
        i = self.rr % self.n
        self.rr += 1
        self.k.wmax("tensor", self.free[i])
        self.free[i] = []
        sl = self.ap[:, i * self.w:(i + 1) * self.w if cols is None else i * self.w + cols]
        return i, sl

    def sl(self, i, c0, c1):
        return self.ap[:, i * self.w + c0:i * self.w + c1]

    def rel(self, i, evt):
        self.free[i].append(evt)


def build_nc(tokens, debug=False, stages='ABCDEFGH', a_level=4):
    M = min(1024, tokens)
    NMT = tokens // M
    NT = M // 128
    NN = max(1, M // 512)
    NW = min(512, M)
    SM = 8 * M

    nc = bass.Bass("TRN2", target_bir_lowering=False,
                   detect_race_conditions=False)
    k = K()

    x_d = nc.dram_tensor("x", [tokens, D], F32, kind="ExternalInput").ap()
    y_d = nc.dram_tensor("y", [tokens, D], F32, kind="ExternalOutput").ap()
    wqkv_d = nc.dram_tensor("wqkv", [D, 3 * D], F32, kind="ExternalInput").ap()
    woutT_d = nc.dram_tensor("woutT", [D, D], F32, kind="ExternalInput").ap()
    gwT_d = nc.dram_tensor("gwT", [D, EXP], F32, kind="ExternalInput").ap()
    w1T_d = nc.dram_tensor("w1T", [EXP, D, D], F32, kind="ExternalInput").ap()
    w2T_d = nc.dram_tensor("w2T", [EXP, D, D], F32, kind="ExternalInput").ap()
    w3T_d = nc.dram_tensor("w3T", [EXP, D, D], F32, kind="ExternalInput").ap()
    sw1T_d = nc.dram_tensor("sw1T", [D, II], F32, kind="ExternalInput").ap()
    sw2T_d = nc.dram_tensor("sw2T", [D, II], F32, kind="ExternalInput").ap()
    sw3T_d = nc.dram_tensor("sw3T", [II, D], F32, kind="ExternalInput").ap()
    b1T_d = nc.dram_tensor("b1T", [D, EXP], F32, kind="ExternalInput").ap()
    b2T_d = nc.dram_tensor("b2T", [D, EXP], F32, kind="ExternalInput").ap()
    b3T_d = nc.dram_tensor("b3T", [D, EXP], F32, kind="ExternalInput").ap()
    bout_d = nc.dram_tensor("bout", [D], F32, kind="ExternalInput").ap()
    dbg = {}
    if debug:
        for nm, shp in [("dbg_xnf", [128, SM]), ("dbg_logits", [128, NT * 8]),
                        ("dbg_srow", [8, M]), ("dbg_qf", [128, SM]),
                        ("dbg_aof", [128, SM]), ("dbg_acc", [128, SM]),
                        ("dbg_hsb", [128, SM]), ("dbg_zz", [128, 16 * M])]:
            dbg[nm] = nc.dram_tensor(nm, shp, F32, kind="ExternalOutput").ap()

    with contextlib.ExitStack() as st:
        def sb(name, shape, dt):
            return st.enter_context(nc.sbuf_tensor(name, shape, dt))

        def ps(name, shape, dt):
            return st.enter_context(nc.psum_tensor(name, shape, dt))

        scr = sb("scr", [128, 32 * M], BF16)
        qf = scr[:, 0:SM]
        kf = scr[:, SM:2 * SM]
        vtok = scr[:, 2 * SM:2 * SM + NT * 1024]
        aof = scr[:, 3 * SM:4 * SM]
        sxn = scr[:, 0:SM]
        h_sb = scr[:, SM:2 * SM]
        zz = scr[:, 2 * SM:4 * SM]
        xn_f = sb("xn_f", [128, SM], BF16)
        acc = sb("acc", [128, SM], F32)
        wsl = [sb(f"wsl{i}", [128, 8 * 512], BF16) for i in range(NBUF)]
        xa = [sb(f"xa{i}", [128, D], F32) for i in range(2)]
        xfin = [sb(f"xfin{i}", [128, D], F32) for i in range(2)]
        otok = [sb(f"otok{i}", [128, D], F32) for i in range(2)]
        xn32 = sb("xn32", [128, D], F32)
        xntok = [sb(f"xntok{i}", [128, D], F32) for i in range(2)]
        sbc = [sb(f"sbc{i}", [128, M], BF16) for i in range(2)]
        mbc = [sb(f"mbc{i}", [128, M], BF16) for i in range(2)]
        s_em8 = sb("s_em8", [8, M], BF16)
        sel8 = sb("sel8", [8, 8 * 128], BF16)
        s_tok = sb("s_tok", [128, 16], F32)
        logits = sb("logits", [128, NT * 8], F32)
        sm32 = sb("sm32", [128, 128], F32)
        p32 = sb("p32", [128, 128], F32)
        pn_sb = sb("pn_sb", [128, 128], BF16)
        pt_sb = sb("pt_sb", [128, 256], BF16)
        tmp3 = [sb(f"tmp3{i}", [128, 512], F32) for i in range(2)]
        tmp2 = [sb(f"tmp2{i}", [128, 512], F32) for i in range(2)]
        z1t = [sb(f"z1t{i}", [128, 512], BF16) for i in range(2)]
        gw_sb = sb("gw_sb", [128, 8 * EXP], F32)
        b1_sb = sb("b1_sb", [128, 8 * EXP], F32)
        b2_sb = sb("b2_sb", [128, 8 * EXP], F32)
        b3_sb = sb("b3_sb", [128, 8 * EXP], F32)
        bout_sb = sb("bout_sb", [128, 8], F32)
        identf = sb("identf", [128, 128], F32)
        identb = sb("identb", [128, 128], BF16)
        amask = sb("amask", [128, 128], F32)

        zerocol = sb("zerocol", [128, 1], F32)
        colp = sb("colp", [128, 16], F32)
        CSX, CMEAN, CSXX, CEX2, CM2, CVAR, CSRT, CY0, CT1, CT2, CT3, CRSTD, CNRM, CRSUM, CRDEN = range(15)

        def cp(i):
            return colp[:, i:i + 1]

        pmm = ps("pmm", [128, 4 * 512], F32)
        ptr = ps("ptr", [128, 2 * 512], F32)
        pbt = ps("pbt", [128, 1024], BF16)
        psml = ps("psml", [128, 512], F32)

        for s in ("wd0", "wd1", "wd2", "wd3", "ws", "xa0", "xa1", "xf0",
                  "xf1", "od0", "od1", "dbg", "mm", "ev", "sc", "gp"):
            k.sem(s, st, nc)

        mmB = Banks(k, pmm, 4, 512)
        trB = Banks(k, ptr, 2, 512)
        btB = Banks(k, pbt, 2, 512)

        e1 = k.e("gpsimd", lambda E: E.memset(identf[:], 1.0), inc="gp")
        k.w("gpsimd", e1)
        e_idf = k.e("gpsimd", lambda E: E.affine_select(
            identf[:], identf[:], [[-1, 128]], ALU.is_equal, 0.0,
            base=0, channel_multiplier=1), inc="gp")
        e2 = k.e("gpsimd", lambda E: E.memset(amask[:], 0.0), inc="gp")
        k.w("gpsimd", e2)
        e_msk = k.e("gpsimd", lambda E: E.affine_select(
            amask[:], amask[:], [[-1, 128]], ALU.is_ge, -1.0e30,
            base=0, channel_multiplier=1), inc="gp")
        e3 = k.e("gpsimd", lambda E: E.memset(sel8[:], 1.0), inc="gp")
        k.w("gpsimd", e3)
        e_onb = k.e("gpsimd", lambda E: E.affine_select(
            sel8[:].rearrange("p (e i) -> p e i", e=8), 
            sel8[:].rearrange("p (e i) -> p e i", e=8),
            [[-1, 8], [0, 128]], ALU.is_equal, 0.0,
            base=0, channel_multiplier=1), inc="gp")
        k.w("vector", e_idf)
        e_idb = k.e("vector", lambda E: E.tensor_copy(identb[:], identf[:]), inc="ev")
        k.e("vector", lambda E: E.memset(zerocol[:], 0.0))

        def sdma(dst, src):
            return k.e("gpsimd", lambda E, d=dst, s=src: E.dma_start(
                out=d, in_=s, allow_slow_non_contiguous=True), inc="ws", amt=16)

        e_gw = sdma(gw_sb[:].rearrange("p (c e) -> p c e", c=8),
                    gwT_d.rearrange("(c p) e -> p c e", p=128))
        e_b1 = sdma(b1_sb[:].rearrange("p (t e) -> p t e", t=8),
                    b1T_d.rearrange("(t p) e -> p t e", p=128))
        e_b2 = sdma(b2_sb[:].rearrange("p (t e) -> p t e", t=8),
                    b2T_d.rearrange("(t p) e -> p t e", p=128))
        e_b3 = sdma(b3_sb[:].rearrange("p (t e) -> p t e", t=8),
                    b3T_d.rearrange("(t p) e -> p t e", p=128))
        e_bo = sdma(bout_sb[:], bout_d.rearrange("(t p) -> p t", p=128))
        e_gw = e_b1 = e_b2 = e_b3 = e_bo = ("ws", k.cnt["ws"])

        class Slabs:
            def __init__(self):
                self.kk = 0
                self.free_evt = [None] * NBUF

            def load(self, src3d):
                buf = self.kk % NBUF
                k.w("gpsimd", self.free_evt[buf])
                evt = k.e("gpsimd", lambda E, a=src3d, b=buf: E.dma_start(
                    out=wsl[b][:].rearrange("p (c n) -> p c n", c=8), in_=a),
                    inc=f"wd{buf}", amt=16)
                self.kk += 1
                return buf, evt

        slabs = Slabs()

        def w2d(ap2d, c0):
            return ap2d[:, c0:c0 + 512].rearrange("(c p) n -> p c n", p=128)

        def w3d(ap3d, e, c0):
            return ap3d[e, :, c0:c0 + 512].rearrange("(c p) n -> p c n", p=128)

        def wrow(ap2d, r0, c0):
            return ap2d[r0:r0 + 1024, c0:c0 + 512].rearrange("(c p) n -> p c n", p=128)

        st_ev = {
            "G_mm_end": None,
            "H_mm_end": None,
            "psml_free": None,
            "attn_mm_end": None,
            "D_mm_end": None,
            "xa_free": [None, None],
            "xfin_free": [None, None],
            "otok_store": [None, None],
            "s_tok_free": [None, None],
            "xntok_free": [None, None],
            "sxn_rd_end": None,
            "tmp_free": [None, None],
            "z1t_free": [None, None],
            "sbc_free": [None, None],
            "gate_prev": None,
        }

        n_store = [0]

        def stage_A(mt):
            xnf_done = None
            x32evts = []
            for tt in range(NT):
                row0 = mt * M + tt * 128
                slot = tt % 2
                k.w("sync", st_ev["xa_free"][slot])
                ld = k.e("sync", lambda E, r=row0, s=slot: E.dma_start(
                    out=xa[s][:], in_=x_d[r:r + 128, :]), inc=f"xa{slot}", amt=16)
                k.w("scalar", ld)
                if tt == 0 and st_ev["G_mm_end"] is not None:
                    k.w("scalar", st_ev["G_mm_end"])
                if st_ev["gate_prev"] is not None:
                    k.w("scalar", st_ev["gate_prev"])
                e_sq = k.e("scalar", lambda E, s=slot: E.activation(
                    xn32[:], xa[s][:], AF.Square, accum_out=cp(CSXX)), inc="sc")
                k.w("vector", ld)
                k.e("vector", lambda E, s=slot: E.reduce_sum(cp(CSX), xa[s][:], axis=AX))
                k.e("vector", lambda E: E.tensor_scalar_mul(cp(CMEAN), cp(CSX), 1.0 / D))
                k.w("vector", e_sq)
                k.e("vector", lambda E: E.tensor_scalar_mul(cp(CEX2), cp(CSXX), 1.0 / D))
                k.e("vector", lambda E: E.tensor_mul(cp(CM2), cp(CMEAN), cp(CMEAN)))
                k.e("vector", lambda E: E.tensor_sub(cp(CVAR), cp(CEX2), cp(CM2)))
                e_var = k.e("vector", lambda E: E.tensor_scalar_add(
                    cp(CT1), cp(CVAR), 1e-5), inc="ev")
                k.w("scalar", e_var)
                e_srt = k.e("scalar", lambda E: E.activation(
                    cp(CSRT), cp(CT1), AF.Sqrt), inc="sc")
                k.w("vector", e_srt)
                e_rc = k.e("vector", lambda E: E.reciprocal(cp(CY0), cp(CSRT)),
                           inc="ev")
                k.w("vector", e_rc)
                k.e("vector", lambda E: E.tensor_mul(cp(CT2), cp(CY0), cp(CY0)))
                k.e("vector", lambda E: E.tensor_mul(cp(CT2), cp(CT2), cp(CT1)))
                k.e("vector", lambda E: E.tensor_scalar(cp(CT3), cp(CT2), -0.5, 1.5, ALU.mult, ALU.add))
                k.e("vector", lambda E: E.tensor_mul(cp(CRSTD), cp(CY0), cp(CT3)))
                k.w("vector", st_ev["xntok_free"][slot])
                e_xn = k.e("vector", lambda E, s=slot: E.tensor_scalar(
                    xntok[s][:], xa[s][:], cp(CMEAN), cp(CRSTD),
                    ALU.subtract, ALU.mult), inc="ev")
                st_ev["xa_free"][slot] = e_xn
                if a_level < 2:
                    continue
                k.w("tensor", e_xn)
                if mt == 0 and tt == 0:
                    k.w("tensor", e_idf)
                tevts = []
                for half in range(2):
                    bi, _ = trB.acq()
                    for cc in range(4):
                        c = half * 4 + cc
                        fn = (lambda E, s=slot, c=c, bi=bi, cc=cc: E.transpose(
                            trB.sl(bi, cc * 128, cc * 128 + 128),
                            xntok[s][:, c * 128:(c + 1) * 128], identf[:]))
                        ev = k.e("tensor", fn, inc="mm" if cc == 3 else None)
                    tevts.append((bi, ev))
                st_ev["xntok_free"][slot] = tevts[1][1]
                if a_level < 3:
                    for bi, ev in tevts:
                        trB.rel(bi, ev)
                    continue
                for half, (bi, ev) in enumerate(tevts):
                    k.w("vector", ev)
                    if half == 0 and st_ev["gate_prev"] is not None:
                        k.w("vector", st_ev["gate_prev"])
                        k.w("vector", xnf_done)
                    e_v = k.e("vector", lambda E, bi=bi, h=half: E.tensor_copy(
                        xn32[:, h * 512:(h + 1) * 512], trB.sl(bi, 0, 512)), inc="ev")
                    k.w("scalar", e_v)
                    e_s = k.e("scalar", lambda E, h=half, tt=tt: E.activation(
                        xn_f[:, 0:SM].rearrange("p (c t) -> p c t", c=8)
                        [:, h * 4:(h + 1) * 4, tt * 128:(tt + 1) * 128],
                        xn32[:, h * 512:(h + 1) * 512].rearrange("p (c t) -> p c t", c=4),
                        AF.Copy), inc="sc")
                    trB.rel(bi, e_v)
                    x32evts.append(e_v)
                    xnf_done = e_s
                if a_level < 4:
                    continue
                k.wmax("tensor", x32evts[-2:])
                if mt == 0 and tt == 0:
                    k.w("tensor", e_gw)
                if st_ev["psml_free"] is not None:
                    k.w("tensor", st_ev["psml_free"])
                for c in range(8):
                    ev = k.e("tensor", lambda E, c=c: E.matmul(
                        psml[:, 0:8], xn32[:, c * 128:(c + 1) * 128],
                        gw_sb[:, c * 8:(c + 1) * 8],
                        start=(c == 0), stop=(c == 7)),
                        inc="mm" if c == 7 else None)
                k.w("vector", ev)
                e_lg = k.e("vector", lambda E, tt=tt: E.tensor_copy(
                    logits[:, tt * 8:(tt + 1) * 8], psml[:, 0:8]), inc="ev")
                st_ev["psml_free"] = e_lg
                st_ev["gate_prev"] = ev
            return xnf_done

        def stage_B(mt, xnf_done):
            qk_done = None
            v_done = None
            for j in range(6):
                buf, rdy = slabs.load(w2d(wqkv_d, j * 512))
                k.w("tensor", rdy)
                if j == 0:
                    k.w("tensor", xnf_done)
                last_mm = None
                if j < 4:
                    dest = qf if j < 2 else kf
                    base_oc = (j % 2) * 4
                    for o in range(4):
                        for n in range(NN):
                            bi, bsl = mmB.acq(cols=NW)
                            for c in range(8):
                                ev = k.e("tensor", lambda E, b=buf, c=c, o=o, n=n, bsl=bsl: E.matmul(
                                    bsl, wsl[b][:, c * 512 + o * 128:c * 512 + (o + 1) * 128],
                                    xn_f[:, c * M + n * 512:c * M + n * 512 + NW],
                                    start=(c == 0), stop=(c == 7)),
                                    inc="mm" if c == 7 else None)
                            last_mm = ev
                            k.w("vector", ev)
                            oc = base_oc + o
                            e_v = k.e("vector", lambda E, bi=bi, oc=oc, n=n, d=dest: E.tensor_copy(
                                d[:, oc * M + n * 512:oc * M + n * 512 + NW],
                                mmB.sl(bi, 0, NW)), inc="ev")
                            mmB.rel(bi, e_v)
                            qk_done = e_v
                else:
                    for tt in range(NT):
                        bi, bsl = mmB.acq()
                        for c in range(8):
                            ev = k.e("tensor", lambda E, b=buf, c=c, tt=tt, bsl=bsl: E.matmul(
                                bsl, xn_f[:, c * M + tt * 128:c * M + (tt + 1) * 128],
                                wsl[b][:, c * 512:(c + 1) * 512],
                                start=(c == 0), stop=(c == 7)),
                                inc="mm" if c == 7 else None)
                        last_mm = ev
                        k.w("vector", ev)
                        e_v = k.e("vector", lambda E, bi=bi, tt=tt, j=j: E.tensor_copy(
                            vtok[:, tt * 1024 + (j - 4) * 512:tt * 1024 + (j - 4) * 512 + 512],
                            mmB.sl(bi, 0, 512)), inc="ev")
                        mmB.rel(bi, e_v)
                        v_done = e_v
                slabs.free_evt[buf] = last_mm
            return qk_done, v_done

        def stage_C(mt, qk_done, v_done):
            aof_done = None
            attn_end = None
            for b in range(NT):
                bi, _ = mmB.acq(cols=128)
                if b == 0:
                    k.w("tensor", qk_done)
                for c in range(8):
                    ev = k.e("tensor", lambda E, c=c, b=b, bi=bi: E.matmul(
                        mmB.sl(bi, 0, 128),
                        qf[:, c * M + b * 128:c * M + (b + 1) * 128],
                        kf[:, c * M + b * 128:c * M + (b + 1) * 128],
                        start=(c == 0), stop=(c == 7)),
                        inc="mm" if c == 7 else None)
                attn_end = ev
                k.w("vector", ev)
                if b == 0 and mt == 0:
                    k.w("vector", (("gp", k.cnt["gp"])))
                k.e("vector", lambda E, bi=bi: E.scalar_tensor_tensor(
                    sm32[:], mmB.sl(bi, 0, 128), SCALE, amask[:],
                    ALU.mult, ALU.add))
                e_nm = k.e("vector", lambda E: E.reduce_max(
                    cp(CNRM), sm32[:], axis=AX, negate=True), inc="ev")
                mmB.rel(bi, e_nm)
                k.w("scalar", e_nm)
                e_p = k.e("scalar", lambda E: E.activation(
                    p32[:], sm32[:], AF.Exp, bias=cp(CNRM),
                    accum_out=cp(CRSUM)), inc="sc")
                k.w("vector", e_p)
                k.e("vector", lambda E: E.tensor_scalar_add(cp(CRDEN), cp(CRSUM), 1e-6))
                e_rc = k.e("vector", lambda E: E.reciprocal(cp(CRDEN), cp(CRDEN)),
                           inc="ev")
                k.w("vector", e_rc)
                e_pn = k.e("vector", lambda E: E.tensor_scalar_mul(
                    pn_sb[:], p32[:], cp(CRDEN)), inc="ev")
                sloti = b % 2
                k.w("tensor", e_pn)
                if mt == 0 and b < 2:
                    k.w("tensor", e_idb)
                k.wmax("tensor", btB.free[sloti])
                btB.free[sloti] = []
                e_tp = k.e("tensor", lambda E, s=sloti: E.transpose(
                    pbt[:, s * 512:s * 512 + 128], pn_sb[:], identb[:]), inc="mm")
                k.w("vector", e_tp)
                e_pt = k.e("vector", lambda E, s=sloti: E.tensor_copy(
                    pt_sb[:, s * 128:(s + 1) * 128], pbt[:, s * 512:s * 512 + 128]),
                    inc="ev")
                btB.free[sloti].append(e_pt)
                k.w("tensor", e_pt)
                if b == 0:
                    k.w("tensor", v_done)
                for half in range(2):
                    bi2, _ = mmB.acq()
                    for dt in range(4):
                        ev = k.e("tensor", lambda E, bi2=bi2, dt=dt, half=half, b=b, s=sloti: E.matmul(
                            mmB.sl(bi2, dt * 128, (dt + 1) * 128),
                            vtok[:, b * 1024 + (half * 4 + dt) * 128:b * 1024 + (half * 4 + dt + 1) * 128],
                            pt_sb[:, s * 128:(s + 1) * 128],
                            start=True, stop=True),
                            inc="mm" if dt == 3 else None)
                    attn_end = ev
                    k.w("scalar", ev)
                    e_a = k.e("scalar", lambda E, bi2=bi2, half=half, b=b: E.activation(
                        aof[:, 0:SM].rearrange("p (c t) -> p c t", c=8)
                        [:, half * 4:(half + 1) * 4, b * 128:(b + 1) * 128],
                        mmB.sl(bi2, 0, 512).rearrange("p (c t) -> p c t", c=4),
                        AF.Copy), inc="sc")
                    mmB.rel(bi2, e_a)
                    aof_done = e_a
            st_ev["attn_mm_end"] = attn_end
            return aof_done

        def stage_D(mt, aof_done):
            D_end = None
            acc_evts = []
            for j in range(2):
                buf, rdy = slabs.load(w2d(woutT_d, j * 512))
                k.w("tensor", rdy)
                if j == 0:
                    k.w("tensor", aof_done)
                last_mm = None
                for o in range(4):
                    for n in range(NN):
                        bi, bsl = mmB.acq(cols=NW)
                        for c in range(8):
                            ev = k.e("tensor", lambda E, b=buf, c=c, o=o, n=n, bsl=bsl: E.matmul(
                                bsl, wsl[b][:, c * 512 + o * 128:c * 512 + (o + 1) * 128],
                                aof[:, c * M + n * 512:c * M + n * 512 + NW],
                                start=(c == 0), stop=(c == 7)),
                                inc="mm" if c == 7 else None)
                        last_mm = ev
                        D_end = ev
                        oc = j * 4 + o
                        k.w("scalar", ev)
                        if j == 0 and o == 0 and n == 0:
                            if st_ev["H_mm_end"] is not None:
                                k.w("scalar", st_ev["H_mm_end"])
                            if mt == 0:
                                k.w("scalar", e_bo)
                        e_s = k.e("scalar", lambda E, bi=bi, oc=oc, n=n: E.activation(
                            acc[:, oc * M + n * 512:oc * M + n * 512 + NW],
                            mmB.sl(bi, 0, NW), AF.Identity,
                            bias=bout_sb[:, oc:oc + 1]), inc="sc")
                        mmB.rel(bi, e_s)
                        acc_evts.append(e_s)
                slabs.free_evt[buf] = last_mm
            st_ev["D_mm_end"] = D_end
            return acc_evts[-1]

        def stage_E(mt):
            srow_done = None
            for tt in range(NT):
                slot = tt % 2
                e_nm = k.e("vector", lambda E, tt=tt: E.reduce_max(
                    cp(CNRM), logits[:, tt * 8:(tt + 1) * 8], axis=AX, negate=True),
                    inc="ev")
                k.w("scalar", e_nm)
                e_p = k.e("scalar", lambda E, tt=tt: E.activation(
                    p32[:, 0:8],
                    logits[:, tt * 8:(tt + 1) * 8], AF.Exp, bias=cp(CNRM),
                    accum_out=cp(CRSUM)), inc="sc")
                k.w("vector", e_p)
                e_rc = k.e("vector", lambda E: E.reciprocal(cp(CRDEN), cp(CRSUM)),
                           inc="ev")
                k.w("vector", e_rc)
                k.e("vector", lambda E: E.tensor_scalar_mul(
                    sm32[:, 0:8], p32[:, 0:8], cp(CRDEN)))
                k.e("vector", lambda E: E.reduce_max(cp(CT1), sm32[:, 0:8], axis=AX))
                k.e("vector", lambda E: E.tensor_single_scalar(
                    sm32[:, 8:16], sm32[:, 0:8], cp(CT1), ALU.is_ge))
                k.e("vector", lambda E: E.scalar_tensor_tensor(
                    sm32[:, 16:24], sm32[:, 8:16], -1.0e30, sm32[:, 0:8],
                    ALU.mult, ALU.add))
                k.e("vector", lambda E: E.reduce_max(cp(CT2), sm32[:, 16:24], axis=AX))
                k.e("vector", lambda E: E.tensor_single_scalar(
                    sm32[:, 8:16], sm32[:, 0:8], cp(CT2), ALU.is_ge))
                if st_ev["s_tok_free"][slot] is not None:
                    k.w("vector", st_ev["s_tok_free"][slot])
                e_s = k.e("vector", lambda E, slot=slot: E.tensor_mul(
                    s_tok[:, slot * 8:slot * 8 + 8], sm32[:, 0:8], sm32[:, 8:16]),
                    inc="ev")
                k.w("tensor", e_s)
                if st_ev["psml_free"] is not None:
                    k.w("tensor", st_ev["psml_free"])
                ev = k.e("tensor", lambda E, slot=slot: E.transpose(
                    psml[0:8, 0:128],
                    s_tok[:, slot * 8:slot * 8 + 8],
                    identf[:]), inc="mm")
                k.w("vector", ev)
                e_r = k.e("vector", lambda E, tt=tt: E.tensor_copy(
                    s_em8[0:8, tt * 128:(tt + 1) * 128],
                    psml[0:8, 0:128]), inc="ev")
                st_ev["psml_free"] = e_r
                st_ev["s_tok_free"][slot] = ev
                srow_done = e_r
            return srow_done

        def stage_F(mt, srow_done, xnf_done):
            pi = [0]

            for e in range(EXP):
                slot = e % 2
                if e == 0:
                    k.w("tensor", srow_done)
                    if mt == 0:
                        k.w("tensor", e_onb)
                sb_evts = []
                for n in range(NN):
                    bi, bsl = mmB.acq(cols=NW)
                    ev = k.e("tensor", lambda E, e=e, n=n, bsl=bsl: E.matmul(
                        bsl, sel8[0:8, e * 128:(e + 1) * 128],
                        s_em8[0:8, n * 512:n * 512 + NW],
                        start=True, stop=True), inc="mm")
                    k.w("vector", ev)
                    if st_ev["sbc_free"][slot] is not None and n == 0:
                        k.w("vector", st_ev["sbc_free"][slot])
                    e_v = k.e("vector", lambda E, bi=bi, n=n, slot=slot: E.tensor_copy(
                        sbc[slot][:, n * 512:n * 512 + NW], mmB.sl(bi, 0, NW)),
                        inc="ev")
                    mmB.rel(bi, e_v)
                    sb_evts.append(e_v)
                e_m = k.e("vector", lambda E, slot=slot: E.tensor_single_scalar(
                    mbc[slot][:], sbc[slot][:], zerocol[:, 0:1], ALU.is_gt), inc="ev")
                if e == 0:
                    k.w("vector", st_ev["attn_mm_end"])
                    k.w("vector", st_ev["D_mm_end"])
                else:
                    k.w("vector", st_ev["sxn_rd_end"])
                e_sxn = None
                for c in range(8):
                    e_sxn = k.e("vector", lambda E, c=c, slot=slot: E.tensor_mul(
                        sxn[:, c * M:(c + 1) * M], xn_f[:, c * M:(c + 1) * M],
                        sbc[slot][:]), inc="ev" if c == 7 else None)
                h_done = None
                last_mm = None
                for j in range(2):
                    buf, rdy = slabs.load(w3d(w1T_d, e, j * 512))
                    k.w("tensor", rdy)
                    if j == 0:
                        k.w("tensor", e_sxn)
                        if e == 0:
                            k.w("tensor", xnf_done)
                    for o in range(4):
                        oc = j * 4 + o
                        for n in range(NN):
                            bi, bsl = mmB.acq(cols=NW)
                            for c in range(8):
                                ev = k.e("tensor", lambda E, b=buf, c=c, o=o, n=n, bsl=bsl: E.matmul(
                                    bsl, wsl[b][:, c * 512 + o * 128:c * 512 + (o + 1) * 128],
                                    sxn[:, c * M + n * 512:c * M + n * 512 + NW],
                                    start=(c == 0), stop=(c == 7)),
                                    inc="mm" if c == 7 else None)
                            last_mm = ev
                            k.w("scalar", ev)
                            if mt == 0 and e == 0 and oc == 0 and n == 0:
                                k.w("scalar", e_b1)
                                k.w("scalar", e_b2)
                                k.w("vector", e_b3)
                            e_h = k.e("scalar", lambda E, bi=bi, oc=oc, n=n, e=e: E.activation(
                                h_sb[:, oc * M + n * 512:oc * M + n * 512 + NW],
                                mmB.sl(bi, 0, NW), AF.Silu,
                                bias=b1_sb[:, oc * 8 + e:oc * 8 + e + 1]), inc="sc")
                            mmB.rel(bi, e_h)
                            h_done = e_h
                    slabs.free_evt[buf] = last_mm
                w3bufs = {}
                w2bufs = {}
                for oc in range(8):
                    j = oc // 4
                    if j not in w3bufs:
                        w3bufs[j] = slabs.load(w3d(w3T_d, e, j * 512))
                        w2bufs[j] = slabs.load(w3d(w2T_d, e, j * 512))
                    b3buf, rdy3 = w3bufs[j]
                    b2buf, rdy2 = w2bufs[j]
                    for n in range(NN):
                        bi3, bsl3 = mmB.acq(cols=NW)
                        if oc % 4 == 0 and n == 0:
                            k.w("tensor", rdy3)
                        for c in range(8):
                            ev3 = k.e("tensor", lambda E, b=b3buf, c=c, oc=oc, n=n, bsl=bsl3: E.matmul(
                                bsl, wsl[b][:, c * 512 + (oc % 4) * 128:c * 512 + (oc % 4 + 1) * 128],
                                sxn[:, c * M + n * 512:c * M + n * 512 + NW],
                                start=(c == 0), stop=(c == 7)),
                                inc="mm" if c == 7 else None)
                        bi2, bsl2 = mmB.acq(cols=NW)
                        if oc % 4 == 0 and n == 0:
                            k.w("tensor", rdy2)
                        if oc == 0 and n == 0:
                            k.w("tensor", h_done)
                        for c in range(8):
                            ev2 = k.e("tensor", lambda E, b=b2buf, c=c, oc=oc, n=n, bsl=bsl2: E.matmul(
                                bsl, wsl[b][:, c * 512 + (oc % 4) * 128:c * 512 + (oc % 4 + 1) * 128],
                                h_sb[:, c * M + n * 512:c * M + n * 512 + NW],
                                start=(c == 0), stop=(c == 7)),
                                inc="mm" if c == 7 else None)
                        i2 = pi[0] % 2
                        k.w("vector", ev3)
                        k.w("vector", e_m)
                        if st_ev["tmp_free"][i2] is not None:
                            k.w("vector", st_ev["tmp_free"][i2])
                        e_x3 = k.e("vector", lambda E, bi3=bi3, oc=oc, n=n, e=e, i2=i2, slot=slot: E.scalar_tensor_tensor(
                            tmp3[i2][:, 0:NW], mmB.sl(bi3, 0, NW),
                            b3_sb[:, oc * 8 + e:oc * 8 + e + 1],
                            mbc[slot][:, n * 512:n * 512 + NW],
                            ALU.add, ALU.mult), inc="ev")
                        mmB.rel(bi3, e_x3)
                        k.w("scalar", ev2)
                        if st_ev["tmp_free"][i2] is not None:
                            k.w("scalar", st_ev["tmp_free"][i2])
                        e_x2 = k.e("scalar", lambda E, bi2=bi2, oc=oc, n=n, e=e, i2=i2: E.activation(
                            tmp2[i2][:, 0:NW], mmB.sl(bi2, 0, NW), AF.Identity,
                            bias=b2_sb[:, oc * 8 + e:oc * 8 + e + 1]), inc="sc")
                        mmB.rel(bi2, e_x2)
                        k.w("vector", e_x2)
                        k.e("vector", lambda E, i2=i2: E.tensor_mul(
                            tmp3[i2][:, 0:NW], tmp3[i2][:, 0:NW], tmp2[i2][:, 0:NW]))
                        e_acc = k.e("vector", lambda E, oc=oc, n=n, i2=i2: E.tensor_add(
                            acc[:, oc * M + n * 512:oc * M + n * 512 + NW],
                            acc[:, oc * M + n * 512:oc * M + n * 512 + NW],
                            tmp3[i2][:, 0:NW]), inc="ev")
                        st_ev["tmp_free"][i2] = e_acc
                        pi[0] += 1
                    if oc % 4 == 3:
                        slabs.free_evt[b3buf] = ev3
                        slabs.free_evt[b2buf] = ev2
                st_ev["sbc_free"][slot] = ev3
                st_ev["sxn_rd_end"] = ev3
            return

        def stage_G(mt):
            zi = [0]
            zz_done = None
            for j in range(4):
                b1buf, rdy1 = slabs.load(w2d(sw1T_d, j * 512))
                b2buf, rdy2 = slabs.load(w2d(sw2T_d, j * 512))
                for it_in in range(4):
                    it = j * 4 + it_in
                    for n in range(NN):
                        bia, bsla = mmB.acq(cols=NW)
                        if it_in == 0 and n == 0:
                            k.w("tensor", rdy1)
                        for c in range(8):
                            ev1 = k.e("tensor", lambda E, b=b1buf, c=c, it_in=it_in, n=n, bsl=bsla: E.matmul(
                                bsl, wsl[b][:, c * 512 + it_in * 128:c * 512 + (it_in + 1) * 128],
                                xn_f[:, c * M + n * 512:c * M + n * 512 + NW],
                                start=(c == 0), stop=(c == 7)),
                                inc="mm" if c == 7 else None)
                        bib, bslb = mmB.acq(cols=NW)
                        if it_in == 0 and n == 0:
                            k.w("tensor", rdy2)
                        for c in range(8):
                            ev2 = k.e("tensor", lambda E, b=b2buf, c=c, it_in=it_in, n=n, bsl=bslb: E.matmul(
                                bsl, wsl[b][:, c * 512 + it_in * 128:c * 512 + (it_in + 1) * 128],
                                xn_f[:, c * M + n * 512:c * M + n * 512 + NW],
                                start=(c == 0), stop=(c == 7)),
                                inc="mm" if c == 7 else None)
                        i2 = zi[0] % 2
                        k.w("scalar", ev1)
                        if st_ev["z1t_free"][i2] is not None:
                            k.w("scalar", st_ev["z1t_free"][i2])
                        e_z1 = k.e("scalar", lambda E, bia=bia, i2=i2: E.activation(
                            z1t[i2][:, 0:NW], mmB.sl(bia, 0, NW), AF.Silu), inc="sc")
                        mmB.rel(bia, e_z1)
                        k.w("vector", ev2)
                        k.w("vector", e_z1)
                        if it == 0 and n == 0:
                            k.w("vector", st_ev["D_mm_end"])
                        e_zz = k.e("vector", lambda E, bib=bib, it=it, n=n, i2=i2: E.tensor_mul(
                            zz[:, it * M + n * 512:it * M + n * 512 + NW],
                            z1t[i2][:, 0:NW], mmB.sl(bib, 0, NW)), inc="ev")
                        mmB.rel(bib, e_zz)
                        st_ev["z1t_free"][i2] = e_zz
                        zz_done = e_zz
                        zi[0] += 1
                slabs.free_evt[b1buf] = ev1
                slabs.free_evt[b2buf] = ev2
            G_end = None
            acc_done = None
            for j in range(2):
                br0, rdy0 = slabs.load(wrow(sw3T_d, 0, j * 512))
                br1, rdy1 = slabs.load(wrow(sw3T_d, 1024, j * 512))
                for o in range(4):
                    oc = j * 4 + o
                    for n in range(NN):
                        bi, bsl = mmB.acq(cols=NW)
                        if o == 0 and n == 0:
                            k.w("tensor", rdy0)
                            k.w("tensor", rdy1)
                            k.w("tensor", zz_done)
                        for half, bb in ((0, br0), (1, br1)):
                            for c in range(8):
                                i = half * 8 + c
                                ev = k.e("tensor", lambda E, b=bb, c=c, o=o, n=n, i=i, bsl=bsl: E.matmul(
                                    bsl, wsl[b][:, c * 512 + o * 128:c * 512 + (o + 1) * 128],
                                    zz[:, i * M + n * 512:i * M + n * 512 + NW],
                                    start=(i == 0), stop=(i == 15)),
                                    inc="mm" if i == 15 else None)
                        G_end = ev
                        k.w("vector", ev)
                        e_a = k.e("vector", lambda E, bi=bi, oc=oc, n=n: E.tensor_add(
                            acc[:, oc * M + n * 512:oc * M + n * 512 + NW],
                            acc[:, oc * M + n * 512:oc * M + n * 512 + NW],
                            mmB.sl(bi, 0, NW)), inc="ev")
                        mmB.rel(bi, e_a)
                        acc_done = e_a
                slabs.free_evt[br0] = G_end
                slabs.free_evt[br1] = G_end
            st_ev["G_mm_end"] = G_end
            return acc_done

        def stage_H(mt, acc_done):
            H_end = None
            for tt in range(NT):
                row0 = mt * M + tt * 128
                slot = tt % 2
                k.w("sync", st_ev["xfin_free"][slot])
                ld = k.e("sync", lambda E, r=row0, s=slot: E.dma_start(
                    out=xfin[s][:], in_=x_d[r:r + 128, :]), inc=f"xf{slot}", amt=16)
                k.w("tensor", acc_done)
                tevts = []
                for half in range(2):
                    bi, _ = trB.acq()
                    for cc in range(4):
                        c = half * 4 + cc
                        ev = k.e("tensor", lambda E, c=c, tt=tt, bi=bi, cc=cc: E.transpose(
                            trB.sl(bi, cc * 128, cc * 128 + 128),
                            acc[:, c * M + tt * 128:c * M + (tt + 1) * 128],
                            identf[:]), inc="mm" if cc == 3 else None)
                    tevts.append((bi, ev))
                H_end = ev
                k.w("vector", ld)
                if st_ev["otok_store"][slot] is not None:
                    k.w("vector", st_ev["otok_store"][slot])
                for half, (bi, ev) in enumerate(tevts):
                    k.w("vector", ev)
                    e_o = k.e("vector", lambda E, bi=bi, h=half, s=slot: E.tensor_add(
                        otok[s][:, h * 512:(h + 1) * 512], trB.sl(bi, 0, 512),
                        xfin[s][:, h * 512:(h + 1) * 512]), inc="ev")
                    trB.rel(bi, e_o)
                st_ev["xfin_free"][slot] = e_o
                k.w("sync", e_o)
                st_out = k.e("sync", lambda E, r=row0, s=slot: E.dma_start(
                    out=y_d[r:r + 128, :], in_=otok[s][:]), inc=f"od{slot}", amt=16)
                st_ev["otok_store"][slot] = st_out
                n_store[0] += 1
            st_ev["H_mm_end"] = H_end

        for mt in range(NMT):
            xnf_done = qk_done = v_done = aof_done = srow_done = acc_done = None
            if "A" in stages:
                xnf_done = stage_A(mt)
            if "B" in stages:
                qk_done, v_done = stage_B(mt, xnf_done)
            if "C" in stages:
                aof_done = stage_C(mt, qk_done, v_done)
            if "D" in stages:
                stage_D(mt, aof_done)
            if "E" in stages:
                srow_done = stage_E(mt)
            if "F" in stages:
                stage_F(mt, srow_done, xnf_done)
            if "G" in stages:
                acc_done = stage_G(mt)
            if "H" in stages:
                stage_H(mt, acc_done)

        if debug:
            k.w("gpsimd", ("od0", k.cnt["od0"]))
            k.w("gpsimd", ("od1", k.cnt["od1"]))
            k.e("gpsimd", lambda E: E.dma_start(out=dbg["dbg_xnf"], in_=xn_f[:]),
                inc="dbg", amt=16)
            k.e("gpsimd", lambda E: E.dma_start(out=dbg["dbg_logits"], in_=logits[:]),
                inc="dbg", amt=16)
            k.e("gpsimd", lambda E: E.dma_start(out=dbg["dbg_srow"], in_=s_em8[:]),
                inc="dbg", amt=16)
            k.e("gpsimd", lambda E: E.dma_start(out=dbg["dbg_qf"], in_=scr[:, 0:SM]),
                inc="dbg", amt=16)
            k.e("gpsimd", lambda E: E.dma_start(out=dbg["dbg_aof"], in_=scr[:, 3 * SM:4 * SM]),
                inc="dbg", amt=16)
            k.e("gpsimd", lambda E: E.dma_start(out=dbg["dbg_acc"], in_=acc[:]),
                inc="dbg", amt=16)
            k.e("gpsimd", lambda E: E.dma_start(out=dbg["dbg_hsb"], in_=scr[:, SM:2 * SM]),
                inc="dbg", amt=16)
            k.e("gpsimd", lambda E: E.dma_start(out=dbg["dbg_zz"], in_=scr[:, 2 * SM:4 * SM]),
                inc="dbg", amt=16)
            k.w("gpsimd", ("dbg", k.cnt["dbg"]))

        k.w("sync", ("od0", k.cnt["od0"]))
        k.w("sync", ("od1", k.cnt["od1"]))

        with nc.Block() as block:
            @block.tensor
            def _(E):
                for f in k.plan["tensor"]:
                    f(E)

            @block.vector
            def _(E):
                for f in k.plan["vector"]:
                    f(E)

            @block.scalar
            def _(E):
                for f in k.plan["scalar"]:
                    f(E)

            @block.gpsimd
            def _(E):
                for f in k.plan["gpsimd"]:
                    f(E)

            @block.sync
            def _(E):
                for f in k.plan["sync"]:
                    f(E)

    return nc


def make_inmaps(inputs, tokens_per_core, n_cores):
    x = np.ascontiguousarray(np.asarray(inputs["L0_attn_x"], dtype=np.float32)
                             .reshape(-1, D))
    wqkv = np.ascontiguousarray(np.asarray(inputs["L0_attn_W_qkv"], np.float32))
    woutT = np.ascontiguousarray(np.asarray(inputs["L0_attn_W_out"], np.float32).T)
    bout = np.ascontiguousarray(np.asarray(inputs["L0_attn_b_out"], np.float32))
    gwT = np.ascontiguousarray(np.asarray(inputs["L0_ffn_gate_w"], np.float32).T)
    w1T = np.ascontiguousarray(np.asarray(inputs["L0_ffn_w1"], np.float32).transpose(0, 2, 1))
    w2T = np.ascontiguousarray(np.asarray(inputs["L0_ffn_w2"], np.float32).transpose(0, 2, 1))
    w3T = np.ascontiguousarray(np.asarray(inputs["L0_ffn_w3"], np.float32).transpose(0, 2, 1))
    b1T = np.ascontiguousarray(np.asarray(inputs["L0_ffn_b1"], np.float32).T)
    b2T = np.ascontiguousarray(np.asarray(inputs["L0_ffn_b2"], np.float32).T)
    b3T = np.ascontiguousarray(np.asarray(inputs["L0_ffn_b3"], np.float32).T)
    sw1T = np.ascontiguousarray(np.asarray(inputs["L0_ffn_sw1"], np.float32).T)
    sw2T = np.ascontiguousarray(np.asarray(inputs["L0_ffn_sw2"], np.float32).T)
    sw3T = np.ascontiguousarray(np.asarray(inputs["L0_ffn_sw3"], np.float32).T)
    shared = dict(wqkv=wqkv, woutT=woutT, bout=bout, gwT=gwT, w1T=w1T, w2T=w2T,
                  w3T=w3T, b1T=b1T, b2T=b2T, b3T=b3T, sw1T=sw1T, sw2T=sw2T,
                  sw3T=sw3T)
    in_maps = []
    for c in range(n_cores):
        m = dict(shared)
        m["x"] = np.ascontiguousarray(
            x[c * tokens_per_core:(c + 1) * tokens_per_core])
        in_maps.append(m)
    return in_maps


def kernel(**inputs):
    n_cores = 8
    tokens_per_core = 4096
    in_maps = make_inmaps(inputs, tokens_per_core, n_cores)
    nc = build_nc(tokens_per_core)
    res = run_bass_kernel_spmd(nc, in_maps, core_ids=list(range(n_cores)))
    out = np.concatenate([r["y"] for r in res.results], axis=0)
    return out.reshape(16, 2048, D).astype(np.float32)


# version 03
# speedup vs baseline: 1.0379x; 1.0379x over previous
import contextlib
import numpy as np

import concourse.bass as bass
import concourse.mybir as mybir
from concourse.bass_utils import run_bass_kernel_spmd

F32 = mybir.dt.float32
BF16 = mybir.dt.bfloat16
AX = mybir.AxisListType.X
AF = mybir.ActivationFunctionType
ALU = mybir.AluOpType

D = 1024
EXP = 8
II = 2048
SCALE = 1.0 / 32.0
NBUF = 4
ENGS = ("tensor", "vector", "scalar", "sync", "gpsimd")


class K:

    def __init__(self):
        self.plan = {e: [] for e in ENGS}
        self.cnt = {}
        self.sems = {}

    def sem(self, name, stack, nc):
        self.sems[name] = stack.enter_context(nc.semaphore(name))
        self.cnt[name] = 0

    def w(self, eng, evt):
        if evt is None:
            return
        s, v = evt
        sem = self.sems[s]
        self.plan[eng].append(lambda E, sem=sem, v=v: E.wait_ge(sem, v))

    def wmax(self, eng, evts):
        by = {}
        for e in evts:
            if e is None:
                continue
            by[e[0]] = max(by.get(e[0], 0), e[1])
        for s, v in by.items():
            self.w(eng, (s, v))

    def e(self, eng, fn, inc=None, amt=1):
        if eng == "vector":
            self.w("vector", ("ev", self.cnt["ev"]))
            if inc is None:
                inc, amt = "ev", 1
        if inc is None:
            self.plan[eng].append(fn)
            return None
        self.cnt[inc] += amt
        v = self.cnt[inc]
        sem = self.sems[inc]
        self.plan[eng].append(lambda E, fn=fn, sem=sem, amt=amt: fn(E).then_inc(sem, amt))
        return (inc, v)


class Banks:

    def __init__(self, k, ap, nbanks, width):
        self.k, self.ap, self.n, self.w = k, ap, nbanks, width
        self.free = [[] for _ in range(nbanks)]
        self.rr = 0

    def acq(self, cols=None):
        i = self.rr % self.n
        self.rr += 1
        self.k.wmax("tensor", self.free[i])
        self.free[i] = []
        sl = self.ap[:, i * self.w:(i + 1) * self.w if cols is None else i * self.w + cols]
        return i, sl

    def sl(self, i, c0, c1):
        return self.ap[:, i * self.w + c0:i * self.w + c1]

    def rel(self, i, evt):
        self.free[i].append(evt)


def build_nc(tokens, debug=False, stages='ABCDEFGH', a_level=4):
    M = min(1024, tokens)
    NMT = tokens // M
    NT = M // 128
    NN = max(1, M // 512)
    NW = min(512, M)
    SM = 8 * M

    nc = bass.Bass("TRN2", target_bir_lowering=False,
                   detect_race_conditions=False)
    k = K()

    x_d = nc.dram_tensor("x", [tokens, D], F32, kind="ExternalInput").ap()
    y_d = nc.dram_tensor("y", [tokens, D], F32, kind="ExternalOutput").ap()
    wqkv_d = nc.dram_tensor("wqkv", [D, 3 * D], F32, kind="ExternalInput").ap()
    woutT_d = nc.dram_tensor("woutT", [D, D], F32, kind="ExternalInput").ap()
    gwT_d = nc.dram_tensor("gwT", [D, EXP], F32, kind="ExternalInput").ap()
    w1T_d = nc.dram_tensor("w1T", [EXP, D, D], F32, kind="ExternalInput").ap()
    w2T_d = nc.dram_tensor("w2T", [EXP, D, D], F32, kind="ExternalInput").ap()
    w3T_d = nc.dram_tensor("w3T", [EXP, D, D], F32, kind="ExternalInput").ap()
    sw1T_d = nc.dram_tensor("sw1T", [D, II], F32, kind="ExternalInput").ap()
    sw2T_d = nc.dram_tensor("sw2T", [D, II], F32, kind="ExternalInput").ap()
    sw3T_d = nc.dram_tensor("sw3T", [II, D], F32, kind="ExternalInput").ap()
    b1T_d = nc.dram_tensor("b1T", [D, EXP], F32, kind="ExternalInput").ap()
    b2T_d = nc.dram_tensor("b2T", [D, EXP], F32, kind="ExternalInput").ap()
    b3T_d = nc.dram_tensor("b3T", [D, EXP], F32, kind="ExternalInput").ap()
    bout_d = nc.dram_tensor("bout", [D], F32, kind="ExternalInput").ap()
    dbg = {}
    if debug:
        for nm, shp in [("dbg_xnf", [128, SM]), ("dbg_logits", [128, NT * 8]),
                        ("dbg_srow", [8, M]), ("dbg_qf", [128, SM]),
                        ("dbg_aof", [128, SM]), ("dbg_acc", [128, SM]),
                        ("dbg_hsb", [128, SM]), ("dbg_zz", [128, 16 * M])]:
            dbg[nm] = nc.dram_tensor(nm, shp, F32, kind="ExternalOutput").ap()

    with contextlib.ExitStack() as st:
        def sb(name, shape, dt):
            return st.enter_context(nc.sbuf_tensor(name, shape, dt))

        def ps(name, shape, dt):
            return st.enter_context(nc.psum_tensor(name, shape, dt))

        scr = sb("scr", [128, 32 * M], BF16)
        qf = scr[:, 0:SM]
        kf = scr[:, SM:2 * SM]
        vtok = scr[:, 2 * SM:2 * SM + NT * 1024]
        aof = scr[:, 3 * SM:4 * SM]
        sxnbuf = [scr[:, 0:SM], scr[:, 3 * SM:4 * SM]]
        h_sb = scr[:, SM:2 * SM]
        zz = scr[:, 2 * SM:4 * SM]
        xn_f = sb("xn_f", [128, SM], BF16)
        acc = sb("acc", [128, SM], F32)
        wsl = [sb(f"wsl{i}", [128, 8 * 512], BF16) for i in range(NBUF)]
        xa = [sb(f"xa{i}", [128, D], F32) for i in range(2)]
        xfin = [sb(f"xfin{i}", [128, D], F32) for i in range(2)]
        otok = [sb(f"otok{i}", [128, D], F32) for i in range(2)]
        xn32 = sb("xn32", [128, D], F32)
        xntok = [sb(f"xntok{i}", [128, D], F32) for i in range(2)]
        sbc = [sb(f"sbc{i}", [128, M], BF16) for i in range(2)]
        mbc = [sb(f"mbc{i}", [128, M], BF16) for i in range(2)]
        s_em8 = sb("s_em8", [8, M], BF16)
        sel8 = sb("sel8", [8, 8 * 128], BF16)
        s_tok = sb("s_tok", [128, 16], F32)
        logits = sb("logits", [128, NT * 8], F32)
        sm32 = sb("sm32", [128, 128], F32)
        p32 = sb("p32", [128, 128], F32)
        pn_sb = sb("pn_sb", [128, 128], BF16)
        pt_sb = sb("pt_sb", [128, 256], BF16)
        tmp3 = [sb(f"tmp3{i}", [128, 512], F32) for i in range(2)]
        tmp2 = [sb(f"tmp2{i}", [128, 512], F32) for i in range(2)]
        z1t = [sb(f"z1t{i}", [128, 512], BF16) for i in range(2)]
        gw_sb = sb("gw_sb", [128, 8 * EXP], F32)
        b1_sb = sb("b1_sb", [128, 8 * EXP], F32)
        b2_sb = sb("b2_sb", [128, 8 * EXP], F32)
        b3_sb = sb("b3_sb", [128, 8 * EXP], F32)
        bout_sb = sb("bout_sb", [128, 8], F32)
        identf = sb("identf", [128, 128], F32)
        identb = sb("identb", [128, 128], BF16)
        amask = sb("amask", [128, 128], F32)

        zerocol = sb("zerocol", [128, 1], F32)
        colp = sb("colp", [128, 16], F32)
        CSX, CMEAN, CSXX, CEX2, CM2, CVAR, CSRT, CY0, CT1, CT2, CT3, CRSTD, CNRM, CRSUM, CRDEN = range(15)

        def cp(i):
            return colp[:, i:i + 1]

        pmm = ps("pmm", [128, 4 * 512], F32)
        ptr = ps("ptr", [128, 2 * 512], F32)
        pbt = ps("pbt", [128, 1024], BF16)
        psml = ps("psml", [128, 512], F32)

        for s in ("wd0", "wd1", "wd2", "wd3", "ws", "xa0", "xa1", "xf0",
                  "xf1", "od0", "od1", "dbg", "mm", "ev", "sc", "gp"):
            k.sem(s, st, nc)

        mmB = Banks(k, pmm, 4, 512)
        trB = Banks(k, ptr, 2, 512)
        btB = Banks(k, pbt, 2, 512)

        e1 = k.e("gpsimd", lambda E: E.memset(identf[:], 1.0), inc="gp")
        k.w("gpsimd", e1)
        e_idf = k.e("gpsimd", lambda E: E.affine_select(
            identf[:], identf[:], [[-1, 128]], ALU.is_equal, 0.0,
            base=0, channel_multiplier=1), inc="gp")
        e2 = k.e("gpsimd", lambda E: E.memset(amask[:], 0.0), inc="gp")
        k.w("gpsimd", e2)
        e_msk = k.e("gpsimd", lambda E: E.affine_select(
            amask[:], amask[:], [[-1, 128]], ALU.is_ge, -1.0e30,
            base=0, channel_multiplier=1), inc="gp")
        e3 = k.e("gpsimd", lambda E: E.memset(sel8[:], 1.0), inc="gp")
        k.w("gpsimd", e3)
        e_onb = k.e("gpsimd", lambda E: E.affine_select(
            sel8[:].rearrange("p (e i) -> p e i", e=8), 
            sel8[:].rearrange("p (e i) -> p e i", e=8),
            [[-1, 8], [0, 128]], ALU.is_equal, 0.0,
            base=0, channel_multiplier=1), inc="gp")
        k.w("vector", e_idf)
        e_idb = k.e("vector", lambda E: E.tensor_copy(identb[:], identf[:]), inc="ev")
        k.e("vector", lambda E: E.memset(zerocol[:], 0.0))

        def sdma(dst, src):
            return k.e("gpsimd", lambda E, d=dst, s=src: E.dma_start(
                out=d, in_=s, allow_slow_non_contiguous=True), inc="ws", amt=16)

        e_gw = sdma(gw_sb[:].rearrange("p (c e) -> p c e", c=8),
                    gwT_d.rearrange("(c p) e -> p c e", p=128))
        e_b1 = sdma(b1_sb[:].rearrange("p (t e) -> p t e", t=8),
                    b1T_d.rearrange("(t p) e -> p t e", p=128))
        e_b2 = sdma(b2_sb[:].rearrange("p (t e) -> p t e", t=8),
                    b2T_d.rearrange("(t p) e -> p t e", p=128))
        e_b3 = sdma(b3_sb[:].rearrange("p (t e) -> p t e", t=8),
                    b3T_d.rearrange("(t p) e -> p t e", p=128))
        e_bo = sdma(bout_sb[:], bout_d.rearrange("(t p) -> p t", p=128))
        e_gw = e_b1 = e_b2 = e_b3 = e_bo = ("ws", k.cnt["ws"])

        class Slabs:
            def __init__(self):
                self.kk = 0
                self.free_evt = [None] * NBUF

            def load(self, src3d):
                buf = self.kk % NBUF
                k.w("gpsimd", self.free_evt[buf])
                evt = k.e("gpsimd", lambda E, a=src3d, b=buf: E.dma_start(
                    out=wsl[b][:].rearrange("p (c n) -> p c n", c=8), in_=a),
                    inc=f"wd{buf}", amt=16)
                self.kk += 1
                return buf, evt

        slabs = Slabs()

        def w2d(ap2d, c0):
            return ap2d[:, c0:c0 + 512].rearrange("(c p) n -> p c n", p=128)

        def w3d(ap3d, e, c0):
            return ap3d[e, :, c0:c0 + 512].rearrange("(c p) n -> p c n", p=128)

        def wrow(ap2d, r0, c0):
            return ap2d[r0:r0 + 1024, c0:c0 + 512].rearrange("(c p) n -> p c n", p=128)

        st_ev = {
            "G_mm_end": None,
            "H_mm_end": None,
            "psml_free": None,
            "attn_mm_end": None,
            "D_mm_end": None,
            "xa_free": [None, None],
            "xfin_free": [None, None],
            "otok_store": [None, None],
            "s_tok_free": [None, None],
            "xntok_free": [None, None],
            "sxn_rd_end": None,
            "sxn_rd_end2": [None, None],
            "tmp_free": [None, None],
            "z1t_free": [None, None],
            "sbc_free": [None, None],
            "gate_prev": None,
        }

        n_store = [0]

        def stage_A(mt):
            xnf_done = None
            x32evts = []
            for tt in range(NT):
                row0 = mt * M + tt * 128
                slot = tt % 2
                k.w("sync", st_ev["xa_free"][slot])
                ld = k.e("sync", lambda E, r=row0, s=slot: E.dma_start(
                    out=xa[s][:], in_=x_d[r:r + 128, :]), inc=f"xa{slot}", amt=16)
                k.w("scalar", ld)
                if st_ev["gate_prev"] is not None:
                    k.w("scalar", st_ev["gate_prev"])
                e_sq = k.e("scalar", lambda E, s=slot: E.activation(
                    xn32[:], xa[s][:], AF.Square, accum_out=cp(CSXX)), inc="sc")
                k.w("vector", ld)
                k.e("vector", lambda E, s=slot: E.reduce_sum(cp(CSX), xa[s][:], axis=AX))
                k.e("vector", lambda E: E.tensor_scalar_mul(cp(CMEAN), cp(CSX), 1.0 / D))
                k.w("vector", e_sq)
                k.e("vector", lambda E: E.tensor_scalar_mul(cp(CEX2), cp(CSXX), 1.0 / D))
                k.e("vector", lambda E: E.tensor_mul(cp(CM2), cp(CMEAN), cp(CMEAN)))
                k.e("vector", lambda E: E.tensor_sub(cp(CVAR), cp(CEX2), cp(CM2)))
                e_var = k.e("vector", lambda E: E.tensor_scalar_add(
                    cp(CT1), cp(CVAR), 1e-5), inc="ev")
                k.w("scalar", e_var)
                e_srt = k.e("scalar", lambda E: E.activation(
                    cp(CSRT), cp(CT1), AF.Sqrt), inc="sc")
                k.w("vector", e_srt)
                e_rc = k.e("vector", lambda E: E.reciprocal(cp(CY0), cp(CSRT)),
                           inc="ev")
                k.w("vector", e_rc)
                k.e("vector", lambda E: E.tensor_mul(cp(CT2), cp(CY0), cp(CY0)))
                k.e("vector", lambda E: E.tensor_mul(cp(CT2), cp(CT2), cp(CT1)))
                k.e("vector", lambda E: E.tensor_scalar(cp(CT3), cp(CT2), -0.5, 1.5, ALU.mult, ALU.add))
                k.e("vector", lambda E: E.tensor_mul(cp(CRSTD), cp(CY0), cp(CT3)))
                k.w("vector", st_ev["xntok_free"][slot])
                e_xn = k.e("vector", lambda E, s=slot: E.tensor_scalar(
                    xntok[s][:], xa[s][:], cp(CMEAN), cp(CRSTD),
                    ALU.subtract, ALU.mult), inc="ev")
                st_ev["xa_free"][slot] = e_xn
                if a_level < 2:
                    continue
                k.w("tensor", e_xn)
                if mt == 0 and tt == 0:
                    k.w("tensor", e_idf)
                tevts = []
                for half in range(2):
                    bi, _ = trB.acq()
                    for cc in range(4):
                        c = half * 4 + cc
                        fn = (lambda E, s=slot, c=c, bi=bi, cc=cc: E.transpose(
                            trB.sl(bi, cc * 128, cc * 128 + 128),
                            xntok[s][:, c * 128:(c + 1) * 128], identf[:]))
                        ev = k.e("tensor", fn, inc="mm" if cc == 3 else None)
                    tevts.append((bi, ev))
                st_ev["xntok_free"][slot] = tevts[1][1]
                if a_level < 3:
                    for bi, ev in tevts:
                        trB.rel(bi, ev)
                    continue
                for half, (bi, ev) in enumerate(tevts):
                    k.w("vector", ev)
                    if half == 0 and st_ev["gate_prev"] is not None:
                        k.w("vector", st_ev["gate_prev"])
                        k.w("vector", xnf_done)
                    e_v = k.e("vector", lambda E, bi=bi, h=half: E.tensor_copy(
                        xn32[:, h * 512:(h + 1) * 512], trB.sl(bi, 0, 512)), inc="ev")
                    k.w("scalar", e_v)
                    if tt == 0 and half == 0 and st_ev["G_mm_end"] is not None:
                        k.w("scalar", st_ev["G_mm_end"])
                    e_s = k.e("scalar", lambda E, h=half, tt=tt: E.activation(
                        xn_f[:, 0:SM].rearrange("p (c t) -> p c t", c=8)
                        [:, h * 4:(h + 1) * 4, tt * 128:(tt + 1) * 128],
                        xn32[:, h * 512:(h + 1) * 512].rearrange("p (c t) -> p c t", c=4),
                        AF.Copy), inc="sc")
                    trB.rel(bi, e_v)
                    x32evts.append(e_v)
                    xnf_done = e_s
                if a_level < 4:
                    continue
                k.wmax("tensor", x32evts[-2:])
                if mt == 0 and tt == 0:
                    k.w("tensor", e_gw)
                if st_ev["psml_free"] is not None:
                    k.w("tensor", st_ev["psml_free"])
                for c in range(8):
                    ev = k.e("tensor", lambda E, c=c: E.matmul(
                        psml[:, 0:8], xn32[:, c * 128:(c + 1) * 128],
                        gw_sb[:, c * 8:(c + 1) * 8],
                        start=(c == 0), stop=(c == 7)),
                        inc="mm" if c == 7 else None)
                k.w("vector", ev)
                e_lg = k.e("vector", lambda E, tt=tt: E.tensor_copy(
                    logits[:, tt * 8:(tt + 1) * 8], psml[:, 0:8]), inc="ev")
                st_ev["psml_free"] = e_lg
                st_ev["gate_prev"] = ev
            return xnf_done

        def stage_B(mt, xnf_done):
            qk_done = None
            v_done = None
            for j in range(6):
                buf, rdy = slabs.load(w2d(wqkv_d, j * 512))
                k.w("tensor", rdy)
                if j == 0:
                    k.w("tensor", xnf_done)
                last_mm = None
                if j < 4:
                    dest = qf if j < 2 else kf
                    base_oc = (j % 2) * 4
                    for o in range(4):
                        for n in range(NN):
                            bi, bsl = mmB.acq(cols=NW)
                            for c in range(8):
                                ev = k.e("tensor", lambda E, b=buf, c=c, o=o, n=n, bsl=bsl: E.matmul(
                                    bsl, wsl[b][:, c * 512 + o * 128:c * 512 + (o + 1) * 128],
                                    xn_f[:, c * M + n * 512:c * M + n * 512 + NW],
                                    start=(c == 0), stop=(c == 7)),
                                    inc="mm" if c == 7 else None)
                            last_mm = ev
                            k.w("vector", ev)
                            oc = base_oc + o
                            e_v = k.e("vector", lambda E, bi=bi, oc=oc, n=n, d=dest: E.tensor_copy(
                                d[:, oc * M + n * 512:oc * M + n * 512 + NW],
                                mmB.sl(bi, 0, NW)), inc="ev")
                            mmB.rel(bi, e_v)
                            qk_done = e_v
                else:
                    for tt in range(NT):
                        bi, bsl = mmB.acq()
                        for c in range(8):
                            ev = k.e("tensor", lambda E, b=buf, c=c, tt=tt, bsl=bsl: E.matmul(
                                bsl, xn_f[:, c * M + tt * 128:c * M + (tt + 1) * 128],
                                wsl[b][:, c * 512:(c + 1) * 512],
                                start=(c == 0), stop=(c == 7)),
                                inc="mm" if c == 7 else None)
                        last_mm = ev
                        k.w("vector", ev)
                        e_v = k.e("vector", lambda E, bi=bi, tt=tt, j=j: E.tensor_copy(
                            vtok[:, tt * 1024 + (j - 4) * 512:tt * 1024 + (j - 4) * 512 + 512],
                            mmB.sl(bi, 0, 512)), inc="ev")
                        mmB.rel(bi, e_v)
                        v_done = e_v
                slabs.free_evt[buf] = last_mm
            return qk_done, v_done

        def stage_C(mt, qk_done, v_done):
            aof_done = None
            attn_end = None
            for b in range(NT):
                bi, _ = mmB.acq(cols=128)
                if b == 0:
                    k.w("tensor", qk_done)
                for c in range(8):
                    ev = k.e("tensor", lambda E, c=c, b=b, bi=bi: E.matmul(
                        mmB.sl(bi, 0, 128),
                        qf[:, c * M + b * 128:c * M + (b + 1) * 128],
                        kf[:, c * M + b * 128:c * M + (b + 1) * 128],
                        start=(c == 0), stop=(c == 7)),
                        inc="mm" if c == 7 else None)
                attn_end = ev
                k.w("vector", ev)
                if b == 0 and mt == 0:
                    k.w("vector", (("gp", k.cnt["gp"])))
                k.e("vector", lambda E, bi=bi: E.scalar_tensor_tensor(
                    sm32[:], mmB.sl(bi, 0, 128), SCALE, amask[:],
                    ALU.mult, ALU.add))
                e_nm = k.e("vector", lambda E: E.reduce_max(
                    cp(CNRM), sm32[:], axis=AX, negate=True), inc="ev")
                mmB.rel(bi, e_nm)
                k.w("scalar", e_nm)
                e_p = k.e("scalar", lambda E: E.activation(
                    p32[:], sm32[:], AF.Exp, bias=cp(CNRM),
                    accum_out=cp(CRSUM)), inc="sc")
                k.w("vector", e_p)
                k.e("vector", lambda E: E.tensor_scalar_add(cp(CRDEN), cp(CRSUM), 1e-6))
                e_rc = k.e("vector", lambda E: E.reciprocal(cp(CRDEN), cp(CRDEN)),
                           inc="ev")
                k.w("vector", e_rc)
                e_pn = k.e("vector", lambda E: E.tensor_scalar_mul(
                    pn_sb[:], p32[:], cp(CRDEN)), inc="ev")
                sloti = b % 2
                k.w("tensor", e_pn)
                if mt == 0 and b < 2:
                    k.w("tensor", e_idb)
                k.wmax("tensor", btB.free[sloti])
                btB.free[sloti] = []
                e_tp = k.e("tensor", lambda E, s=sloti: E.transpose(
                    pbt[:, s * 512:s * 512 + 128], pn_sb[:], identb[:]), inc="mm")
                k.w("vector", e_tp)
                e_pt = k.e("vector", lambda E, s=sloti: E.tensor_copy(
                    pt_sb[:, s * 128:(s + 1) * 128], pbt[:, s * 512:s * 512 + 128]),
                    inc="ev")
                btB.free[sloti].append(e_pt)
                k.w("tensor", e_pt)
                if b == 0:
                    k.w("tensor", v_done)
                for half in range(2):
                    bi2, _ = mmB.acq()
                    for dt in range(4):
                        ev = k.e("tensor", lambda E, bi2=bi2, dt=dt, half=half, b=b, s=sloti: E.matmul(
                            mmB.sl(bi2, dt * 128, (dt + 1) * 128),
                            vtok[:, b * 1024 + (half * 4 + dt) * 128:b * 1024 + (half * 4 + dt + 1) * 128],
                            pt_sb[:, s * 128:(s + 1) * 128],
                            start=True, stop=True),
                            inc="mm" if dt == 3 else None)
                    attn_end = ev
                    k.w("scalar", ev)
                    e_a = k.e("scalar", lambda E, bi2=bi2, half=half, b=b: E.activation(
                        aof[:, 0:SM].rearrange("p (c t) -> p c t", c=8)
                        [:, half * 4:(half + 1) * 4, b * 128:(b + 1) * 128],
                        mmB.sl(bi2, 0, 512).rearrange("p (c t) -> p c t", c=4),
                        AF.Copy), inc="sc")
                    mmB.rel(bi2, e_a)
                    aof_done = e_a
            st_ev["attn_mm_end"] = attn_end
            return aof_done

        def stage_D(mt, aof_done):
            D_end = None
            acc_evts = []
            for j in range(2):
                buf, rdy = slabs.load(w2d(woutT_d, j * 512))
                k.w("tensor", rdy)
                if j == 0:
                    k.w("tensor", aof_done)
                last_mm = None
                for o in range(4):
                    for n in range(NN):
                        bi, bsl = mmB.acq(cols=NW)
                        for c in range(8):
                            ev = k.e("tensor", lambda E, b=buf, c=c, o=o, n=n, bsl=bsl: E.matmul(
                                bsl, wsl[b][:, c * 512 + o * 128:c * 512 + (o + 1) * 128],
                                aof[:, c * M + n * 512:c * M + n * 512 + NW],
                                start=(c == 0), stop=(c == 7)),
                                inc="mm" if c == 7 else None)
                        last_mm = ev
                        D_end = ev
                        oc = j * 4 + o
                        k.w("scalar", ev)
                        if j == 0 and o == 0 and n == 0:
                            if st_ev["H_mm_end"] is not None:
                                k.w("scalar", st_ev["H_mm_end"])
                            if mt == 0:
                                k.w("scalar", e_bo)
                        e_s = k.e("scalar", lambda E, bi=bi, oc=oc, n=n: E.activation(
                            acc[:, oc * M + n * 512:oc * M + n * 512 + NW],
                            mmB.sl(bi, 0, NW), AF.Identity,
                            bias=bout_sb[:, oc:oc + 1]), inc="sc")
                        mmB.rel(bi, e_s)
                        acc_evts.append(e_s)
                slabs.free_evt[buf] = last_mm
            st_ev["D_mm_end"] = D_end
            return acc_evts[-1]

        def stage_E(mt):
            srow_done = None
            for tt in range(NT):
                slot = tt % 2
                e_nm = k.e("vector", lambda E, tt=tt: E.reduce_max(
                    cp(CNRM), logits[:, tt * 8:(tt + 1) * 8], axis=AX, negate=True),
                    inc="ev")
                k.w("scalar", e_nm)
                e_p = k.e("scalar", lambda E, tt=tt: E.activation(
                    p32[:, 0:8],
                    logits[:, tt * 8:(tt + 1) * 8], AF.Exp, bias=cp(CNRM),
                    accum_out=cp(CRSUM)), inc="sc")
                k.w("vector", e_p)
                e_rc = k.e("vector", lambda E: E.reciprocal(cp(CRDEN), cp(CRSUM)),
                           inc="ev")
                k.w("vector", e_rc)
                k.e("vector", lambda E: E.tensor_scalar_mul(
                    sm32[:, 0:8], p32[:, 0:8], cp(CRDEN)))
                k.e("vector", lambda E: E.reduce_max(cp(CT1), sm32[:, 0:8], axis=AX))
                k.e("vector", lambda E: E.tensor_single_scalar(
                    sm32[:, 8:16], sm32[:, 0:8], cp(CT1), ALU.is_ge))
                k.e("vector", lambda E: E.scalar_tensor_tensor(
                    sm32[:, 16:24], sm32[:, 8:16], -1.0e30, sm32[:, 0:8],
                    ALU.mult, ALU.add))
                k.e("vector", lambda E: E.reduce_max(cp(CT2), sm32[:, 16:24], axis=AX))
                k.e("vector", lambda E: E.tensor_single_scalar(
                    sm32[:, 8:16], sm32[:, 0:8], cp(CT2), ALU.is_ge))
                if st_ev["s_tok_free"][slot] is not None:
                    k.w("vector", st_ev["s_tok_free"][slot])
                e_s = k.e("vector", lambda E, slot=slot: E.tensor_mul(
                    s_tok[:, slot * 8:slot * 8 + 8], sm32[:, 0:8], sm32[:, 8:16]),
                    inc="ev")
                k.w("tensor", e_s)
                if st_ev["psml_free"] is not None:
                    k.w("tensor", st_ev["psml_free"])
                ev = k.e("tensor", lambda E, slot=slot: E.transpose(
                    psml[0:8, 0:128],
                    s_tok[:, slot * 8:slot * 8 + 8],
                    identf[:]), inc="mm")
                k.w("vector", ev)
                e_r = k.e("vector", lambda E, tt=tt: E.tensor_copy(
                    s_em8[0:8, tt * 128:(tt + 1) * 128],
                    psml[0:8, 0:128]), inc="ev")
                st_ev["psml_free"] = e_r
                st_ev["s_tok_free"][slot] = ev
                srow_done = e_r
            return srow_done

        def stage_F(mt, srow_done, xnf_done):
            pi = [0]

            def prepare(e):
                slot = e % 2
                if e == 0:
                    k.w("tensor", srow_done)
                    if mt == 0:
                        k.w("tensor", e_onb)
                for n in range(NN):
                    bi, bsl = mmB.acq(cols=NW)
                    ev = k.e("tensor", lambda E, e=e, n=n, bsl=bsl: E.matmul(
                        bsl, sel8[0:8, e * 128:(e + 1) * 128],
                        s_em8[0:8, n * 512:n * 512 + NW],
                        start=True, stop=True), inc="mm")
                    k.w("vector", ev)
                    if st_ev["sbc_free"][slot] is not None and n == 0:
                        k.w("vector", st_ev["sbc_free"][slot])
                    e_v = k.e("vector", lambda E, bi=bi, n=n, slot=slot: E.tensor_copy(
                        sbc[slot][:, n * 512:n * 512 + NW], mmB.sl(bi, 0, NW)),
                        inc="ev")
                    mmB.rel(bi, e_v)
                k.e("vector", lambda E, slot=slot: E.tensor_single_scalar(
                    mbc[slot][:], sbc[slot][:], zerocol[:, 0:1], ALU.is_gt))
                if e == 0:
                    k.w("vector", st_ev["attn_mm_end"])
                elif e == 1:
                    k.w("vector", st_ev["D_mm_end"])
                else:
                    k.w("vector", st_ev["sxn_rd_end2"][slot])
                res = {"e_sxn": None}
                chunks = []
                for c in range(8):
                    def mk(c=c, slot=slot):
                        def em():
                            res["e_sxn"] = k.e("vector", lambda E, c=c, slot=slot: E.tensor_mul(
                                sxnbuf[slot][:, c * M:(c + 1) * M],
                                xn_f[:, c * M:(c + 1) * M], sbc[slot][:]))
                        return em
                    chunks.append(mk())
                return chunks, res

            prep = {0: prepare(0)}
            for e in range(EXP):
                slot = e % 2
                sxn = sxnbuf[slot]
                chunks, res = prep.pop(e)
                for f in chunks:
                    f()
                e_sxn = res["e_sxn"]
                h_done = None
                last_mm = None
                for j in range(2):
                    buf, rdy = slabs.load(w3d(w1T_d, e, j * 512))
                    k.w("tensor", rdy)
                    if j == 0:
                        k.w("tensor", e_sxn)
                        if e == 0:
                            k.w("tensor", xnf_done)
                    for o in range(4):
                        oc = j * 4 + o
                        for n in range(NN):
                            bi, bsl = mmB.acq(cols=NW)
                            for c in range(8):
                                ev = k.e("tensor", lambda E, b=buf, c=c, o=o, n=n, bsl=bsl, sxn=sxn: E.matmul(
                                    bsl, wsl[b][:, c * 512 + o * 128:c * 512 + (o + 1) * 128],
                                    sxn[:, c * M + n * 512:c * M + n * 512 + NW],
                                    start=(c == 0), stop=(c == 7)),
                                    inc="mm" if c == 7 else None)
                            last_mm = ev
                            k.w("scalar", ev)
                            if mt == 0 and e == 0 and oc == 0 and n == 0:
                                k.w("scalar", e_b1)
                                k.w("scalar", e_b2)
                                k.w("vector", e_b3)
                            e_h = k.e("scalar", lambda E, bi=bi, oc=oc, n=n, e=e: E.activation(
                                h_sb[:, oc * M + n * 512:oc * M + n * 512 + NW],
                                mmB.sl(bi, 0, NW), AF.Silu,
                                bias=b1_sb[:, oc * 8 + e:oc * 8 + e + 1]), inc="sc")
                            mmB.rel(bi, e_h)
                            h_done = e_h
                    slabs.free_evt[buf] = last_mm
                w3bufs = {}
                w2bufs = {}
                for oc in range(8):
                    j = oc // 4
                    if j not in w3bufs:
                        w3bufs[j] = slabs.load(w3d(w3T_d, e, j * 512))
                        w2bufs[j] = slabs.load(w3d(w2T_d, e, j * 512))
                    b3buf, rdy3 = w3bufs[j]
                    b2buf, rdy2 = w2bufs[j]
                    for n in range(NN):
                        bi3, bsl3 = mmB.acq(cols=NW)
                        if oc % 4 == 0 and n == 0:
                            k.w("tensor", rdy3)
                        for c in range(8):
                            ev3 = k.e("tensor", lambda E, b=b3buf, c=c, oc=oc, n=n, bsl=bsl3, sxn=sxn: E.matmul(
                                bsl, wsl[b][:, c * 512 + (oc % 4) * 128:c * 512 + (oc % 4 + 1) * 128],
                                sxn[:, c * M + n * 512:c * M + n * 512 + NW],
                                start=(c == 0), stop=(c == 7)),
                                inc="mm" if c == 7 else None)
                        bi2, bsl2 = mmB.acq(cols=NW)
                        if oc % 4 == 0 and n == 0:
                            k.w("tensor", rdy2)
                        if oc == 0 and n == 0:
                            k.w("tensor", h_done)
                        for c in range(8):
                            ev2 = k.e("tensor", lambda E, b=b2buf, c=c, oc=oc, n=n, bsl=bsl2: E.matmul(
                                bsl, wsl[b][:, c * 512 + (oc % 4) * 128:c * 512 + (oc % 4 + 1) * 128],
                                h_sb[:, c * M + n * 512:c * M + n * 512 + NW],
                                start=(c == 0), stop=(c == 7)),
                                inc="mm" if c == 7 else None)
                        i2 = pi[0] % 2
                        k.w("vector", ev3)
                        if st_ev["tmp_free"][i2] is not None:
                            k.w("vector", st_ev["tmp_free"][i2])
                        e_x3 = k.e("vector", lambda E, bi3=bi3, oc=oc, n=n, e=e, i2=i2, slot=slot: E.scalar_tensor_tensor(
                            tmp3[i2][:, 0:NW], mmB.sl(bi3, 0, NW),
                            b3_sb[:, oc * 8 + e:oc * 8 + e + 1],
                            mbc[slot][:, n * 512:n * 512 + NW],
                            ALU.add, ALU.mult), inc="ev")
                        mmB.rel(bi3, e_x3)
                        k.w("scalar", ev2)
                        if st_ev["tmp_free"][i2] is not None:
                            k.w("scalar", st_ev["tmp_free"][i2])
                        e_x2 = k.e("scalar", lambda E, bi2=bi2, oc=oc, n=n, e=e, i2=i2: E.activation(
                            tmp2[i2][:, 0:NW], mmB.sl(bi2, 0, NW), AF.Identity,
                            bias=b2_sb[:, oc * 8 + e:oc * 8 + e + 1]), inc="sc")
                        mmB.rel(bi2, e_x2)
                        k.w("vector", e_x2)
                        k.e("vector", lambda E, i2=i2: E.tensor_mul(
                            tmp3[i2][:, 0:NW], tmp3[i2][:, 0:NW], tmp2[i2][:, 0:NW]))
                        e_acc = k.e("vector", lambda E, oc=oc, n=n, i2=i2: E.tensor_add(
                            acc[:, oc * M + n * 512:oc * M + n * 512 + NW],
                            acc[:, oc * M + n * 512:oc * M + n * 512 + NW],
                            tmp3[i2][:, 0:NW]), inc="ev")
                        st_ev["tmp_free"][i2] = e_acc
                        pi[0] += 1
                        if e + 1 < EXP:
                            if e + 1 not in prep:
                                if oc >= 1:
                                    prep[e + 1] = prepare(e + 1)
                            else:
                                nch = prep[e + 1][0]
                                for _ in range(2):
                                    if nch:
                                        nch.pop(0)()
                    if oc % 4 == 3:
                        slabs.free_evt[b3buf] = ev3
                        slabs.free_evt[b2buf] = ev2
                st_ev["sbc_free"][slot] = ev3
                st_ev["sxn_rd_end"] = ev3
                st_ev["sxn_rd_end2"][slot] = ev3
            return

        def stage_G(mt):
            zi = [0]
            zz_done = None
            for j in range(4):
                b1buf, rdy1 = slabs.load(w2d(sw1T_d, j * 512))
                b2buf, rdy2 = slabs.load(w2d(sw2T_d, j * 512))
                for it_in in range(4):
                    it = j * 4 + it_in
                    for n in range(NN):
                        bia, bsla = mmB.acq(cols=NW)
                        if it_in == 0 and n == 0:
                            k.w("tensor", rdy1)
                        for c in range(8):
                            ev1 = k.e("tensor", lambda E, b=b1buf, c=c, it_in=it_in, n=n, bsl=bsla: E.matmul(
                                bsl, wsl[b][:, c * 512 + it_in * 128:c * 512 + (it_in + 1) * 128],
                                xn_f[:, c * M + n * 512:c * M + n * 512 + NW],
                                start=(c == 0), stop=(c == 7)),
                                inc="mm" if c == 7 else None)
                        bib, bslb = mmB.acq(cols=NW)
                        if it_in == 0 and n == 0:
                            k.w("tensor", rdy2)
                        for c in range(8):
                            ev2 = k.e("tensor", lambda E, b=b2buf, c=c, it_in=it_in, n=n, bsl=bslb: E.matmul(
                                bsl, wsl[b][:, c * 512 + it_in * 128:c * 512 + (it_in + 1) * 128],
                                xn_f[:, c * M + n * 512:c * M + n * 512 + NW],
                                start=(c == 0), stop=(c == 7)),
                                inc="mm" if c == 7 else None)
                        i2 = zi[0] % 2
                        k.w("scalar", ev1)
                        if st_ev["z1t_free"][i2] is not None:
                            k.w("scalar", st_ev["z1t_free"][i2])
                        e_z1 = k.e("scalar", lambda E, bia=bia, i2=i2: E.activation(
                            z1t[i2][:, 0:NW], mmB.sl(bia, 0, NW), AF.Silu), inc="sc")
                        mmB.rel(bia, e_z1)
                        k.w("vector", ev2)
                        k.w("vector", e_z1)
                        if it == 0 and n == 0:
                            k.w("vector", st_ev["D_mm_end"])
                            k.w("vector", st_ev["sxn_rd_end"])
                        e_zz = k.e("vector", lambda E, bib=bib, it=it, n=n, i2=i2: E.tensor_mul(
                            zz[:, it * M + n * 512:it * M + n * 512 + NW],
                            z1t[i2][:, 0:NW], mmB.sl(bib, 0, NW)), inc="ev")
                        mmB.rel(bib, e_zz)
                        st_ev["z1t_free"][i2] = e_zz
                        zz_done = e_zz
                        zi[0] += 1
                slabs.free_evt[b1buf] = ev1
                slabs.free_evt[b2buf] = ev2
            G_end = None
            acc_done = None
            for j in range(2):
                br0, rdy0 = slabs.load(wrow(sw3T_d, 0, j * 512))
                br1, rdy1 = slabs.load(wrow(sw3T_d, 1024, j * 512))
                for o in range(4):
                    oc = j * 4 + o
                    for n in range(NN):
                        bi, bsl = mmB.acq(cols=NW)
                        if o == 0 and n == 0:
                            k.w("tensor", rdy0)
                            k.w("tensor", rdy1)
                            k.w("tensor", zz_done)
                        for half, bb in ((0, br0), (1, br1)):
                            for c in range(8):
                                i = half * 8 + c
                                ev = k.e("tensor", lambda E, b=bb, c=c, o=o, n=n, i=i, bsl=bsl: E.matmul(
                                    bsl, wsl[b][:, c * 512 + o * 128:c * 512 + (o + 1) * 128],
                                    zz[:, i * M + n * 512:i * M + n * 512 + NW],
                                    start=(i == 0), stop=(i == 15)),
                                    inc="mm" if i == 15 else None)
                        G_end = ev
                        k.w("vector", ev)
                        e_a = k.e("vector", lambda E, bi=bi, oc=oc, n=n: E.tensor_add(
                            acc[:, oc * M + n * 512:oc * M + n * 512 + NW],
                            acc[:, oc * M + n * 512:oc * M + n * 512 + NW],
                            mmB.sl(bi, 0, NW)), inc="ev")
                        mmB.rel(bi, e_a)
                        acc_done = e_a
                slabs.free_evt[br0] = G_end
                slabs.free_evt[br1] = G_end
            st_ev["G_mm_end"] = G_end
            return acc_done

        def stage_H(mt, acc_done):
            H_end = None
            for tt in range(NT):
                row0 = mt * M + tt * 128
                slot = tt % 2
                k.w("sync", st_ev["xfin_free"][slot])
                ld = k.e("sync", lambda E, r=row0, s=slot: E.dma_start(
                    out=xfin[s][:], in_=x_d[r:r + 128, :]), inc=f"xf{slot}", amt=16)
                k.w("tensor", acc_done)
                tevts = []
                for half in range(2):
                    bi, _ = trB.acq()
                    for cc in range(4):
                        c = half * 4 + cc
                        ev = k.e("tensor", lambda E, c=c, tt=tt, bi=bi, cc=cc: E.transpose(
                            trB.sl(bi, cc * 128, cc * 128 + 128),
                            acc[:, c * M + tt * 128:c * M + (tt + 1) * 128],
                            identf[:]), inc="mm" if cc == 3 else None)
                    tevts.append((bi, ev))
                H_end = ev
                k.w("vector", ld)
                if st_ev["otok_store"][slot] is not None:
                    k.w("vector", st_ev["otok_store"][slot])
                for half, (bi, ev) in enumerate(tevts):
                    k.w("vector", ev)
                    e_o = k.e("vector", lambda E, bi=bi, h=half, s=slot: E.tensor_add(
                        otok[s][:, h * 512:(h + 1) * 512], trB.sl(bi, 0, 512),
                        xfin[s][:, h * 512:(h + 1) * 512]), inc="ev")
                    trB.rel(bi, e_o)
                st_ev["xfin_free"][slot] = e_o
                k.w("sync", e_o)
                st_out = k.e("sync", lambda E, r=row0, s=slot: E.dma_start(
                    out=y_d[r:r + 128, :], in_=otok[s][:]), inc=f"od{slot}", amt=16)
                st_ev["otok_store"][slot] = st_out
                n_store[0] += 1
            st_ev["H_mm_end"] = H_end

        for mt in range(NMT):
            xnf_done = qk_done = v_done = aof_done = srow_done = acc_done = None
            if "A" in stages:
                xnf_done = stage_A(mt)
            if "B" in stages:
                qk_done, v_done = stage_B(mt, xnf_done)
            if "C" in stages:
                aof_done = stage_C(mt, qk_done, v_done)
            if "D" in stages:
                stage_D(mt, aof_done)
            if "E" in stages:
                srow_done = stage_E(mt)
            if "F" in stages:
                stage_F(mt, srow_done, xnf_done)
            if "G" in stages:
                acc_done = stage_G(mt)
            if "H" in stages:
                stage_H(mt, acc_done)

        if debug:
            k.w("gpsimd", ("od0", k.cnt["od0"]))
            k.w("gpsimd", ("od1", k.cnt["od1"]))
            k.e("gpsimd", lambda E: E.dma_start(out=dbg["dbg_xnf"], in_=xn_f[:]),
                inc="dbg", amt=16)
            k.e("gpsimd", lambda E: E.dma_start(out=dbg["dbg_logits"], in_=logits[:]),
                inc="dbg", amt=16)
            k.e("gpsimd", lambda E: E.dma_start(out=dbg["dbg_srow"], in_=s_em8[:]),
                inc="dbg", amt=16)
            k.e("gpsimd", lambda E: E.dma_start(out=dbg["dbg_qf"], in_=scr[:, 0:SM]),
                inc="dbg", amt=16)
            k.e("gpsimd", lambda E: E.dma_start(out=dbg["dbg_aof"], in_=scr[:, 3 * SM:4 * SM]),
                inc="dbg", amt=16)
            k.e("gpsimd", lambda E: E.dma_start(out=dbg["dbg_acc"], in_=acc[:]),
                inc="dbg", amt=16)
            k.e("gpsimd", lambda E: E.dma_start(out=dbg["dbg_hsb"], in_=scr[:, SM:2 * SM]),
                inc="dbg", amt=16)
            k.e("gpsimd", lambda E: E.dma_start(out=dbg["dbg_zz"], in_=scr[:, 2 * SM:4 * SM]),
                inc="dbg", amt=16)
            k.w("gpsimd", ("dbg", k.cnt["dbg"]))

        k.w("sync", ("od0", k.cnt["od0"]))
        k.w("sync", ("od1", k.cnt["od1"]))

        with nc.Block() as block:
            @block.tensor
            def _(E):
                for f in k.plan["tensor"]:
                    f(E)

            @block.vector
            def _(E):
                for f in k.plan["vector"]:
                    f(E)

            @block.scalar
            def _(E):
                for f in k.plan["scalar"]:
                    f(E)

            @block.gpsimd
            def _(E):
                for f in k.plan["gpsimd"]:
                    f(E)

            @block.sync
            def _(E):
                for f in k.plan["sync"]:
                    f(E)

    return nc


def make_inmaps(inputs, tokens_per_core, n_cores):
    x = np.ascontiguousarray(np.asarray(inputs["L0_attn_x"], dtype=np.float32)
                             .reshape(-1, D))
    wqkv = np.ascontiguousarray(np.asarray(inputs["L0_attn_W_qkv"], np.float32))
    woutT = np.ascontiguousarray(np.asarray(inputs["L0_attn_W_out"], np.float32).T)
    bout = np.ascontiguousarray(np.asarray(inputs["L0_attn_b_out"], np.float32))
    gwT = np.ascontiguousarray(np.asarray(inputs["L0_ffn_gate_w"], np.float32).T)
    w1T = np.ascontiguousarray(np.asarray(inputs["L0_ffn_w1"], np.float32).transpose(0, 2, 1))
    w2T = np.ascontiguousarray(np.asarray(inputs["L0_ffn_w2"], np.float32).transpose(0, 2, 1))
    w3T = np.ascontiguousarray(np.asarray(inputs["L0_ffn_w3"], np.float32).transpose(0, 2, 1))
    b1T = np.ascontiguousarray(np.asarray(inputs["L0_ffn_b1"], np.float32).T)
    b2T = np.ascontiguousarray(np.asarray(inputs["L0_ffn_b2"], np.float32).T)
    b3T = np.ascontiguousarray(np.asarray(inputs["L0_ffn_b3"], np.float32).T)
    sw1T = np.ascontiguousarray(np.asarray(inputs["L0_ffn_sw1"], np.float32).T)
    sw2T = np.ascontiguousarray(np.asarray(inputs["L0_ffn_sw2"], np.float32).T)
    sw3T = np.ascontiguousarray(np.asarray(inputs["L0_ffn_sw3"], np.float32).T)
    shared = dict(wqkv=wqkv, woutT=woutT, bout=bout, gwT=gwT, w1T=w1T, w2T=w2T,
                  w3T=w3T, b1T=b1T, b2T=b2T, b3T=b3T, sw1T=sw1T, sw2T=sw2T,
                  sw3T=sw3T)
    in_maps = []
    for c in range(n_cores):
        m = dict(shared)
        m["x"] = np.ascontiguousarray(
            x[c * tokens_per_core:(c + 1) * tokens_per_core])
        in_maps.append(m)
    return in_maps


def kernel(**inputs):
    n_cores = 8
    tokens_per_core = 4096
    in_maps = make_inmaps(inputs, tokens_per_core, n_cores)
    nc = build_nc(tokens_per_core)
    res = run_bass_kernel_spmd(nc, in_maps, core_ids=list(range(n_cores)))
    out = np.concatenate([r["y"] for r in res.results], axis=0)
    return out.reshape(16, 2048, D).astype(np.float32)
